# version 04
# baseline (speedup 1.0000x reference)
import sys
sys.path.insert(0, "/opt/trn_rl_repo")
import numpy as np
import concourse.bass as bass
import concourse.mybir as mybir
from concourse.bass_utils import run_bass_kernel_spmd

F32 = mybir.dt.float32
BF16 = mybir.dt.bfloat16
I32 = mybir.dt.int32
AF = mybir.ActivationFunctionType
ALU = mybir.AluOpType

NCORES = 8
B, L, D = 4, 8192, 1024
DI = 1536
HALF = L // 2
ENGINES = ("sync", "gpsimd", "tensor", "vector", "scalar")


class Op:
    __slots__ = ("eng", "fn", "deps", "dma", "inc_val", "cum", "dsem", "force")

    def __init__(self, eng, fn, deps, dma, force):
        self.eng, self.fn, self.deps, self.dma = eng, fn, list(deps), dma
        self.force = [f for f in force if f is not None]
        self.inc_val = 0
        self.cum = 0


class Prog:

    def __init__(self):
        self.queues = {e: [] for e in ENGINES}
        self.ops = []

    def op(self, eng, fn, deps=(), dma=False, force=()):
        o = Op(eng, fn, [d for d in deps if d is not None], dma, force)
        self.queues[eng].append(o)
        self.ops.append(o)
        return o

    NDMASEM = 8

    def emit(self, nc, block, ctx):
        for o in self.ops:
            if o.dma:
                o.inc_val = 16
            for d in o.deps:
                if d.eng != o.eng or d.dma:
                    d.inc_val = 16 if d.dma else 1
            for d in o.force:
                if not d.dma:
                    d.inc_val = 1
        for eng in ENGINES:
            c = 0
            dcnt = [0] * self.NDMASEM
            di = 0
            for o in self.queues[eng]:
                if o.dma:
                    o.dsem = di % self.NDMASEM
                    dcnt[o.dsem] += 16
                    o.cum = dcnt[o.dsem]
                    di += 1
                else:
                    c += o.inc_val
                    o.cum = c
        sems = {}
        dsems = {}
        for eng in ENGINES:
            if any(o.inc_val and not o.dma for o in self.queues[eng]):
                sems[eng] = ctx.enter_context(nc.semaphore(f"se_{eng}"))
            if any(o.dma for o in self.queues[eng]):
                dsems[eng] = [ctx.enter_context(nc.semaphore(f"sd_{eng}{i}"))
                              for i in range(self.NDMASEM)]

        def run_engine(engine_obj, eng_name):
            waited = {e: 0 for e in ENGINES}
            dwaited = {e: [0] * self.NDMASEM for e in ENGINES}
            for o in self.queues[eng_name]:
                for d in list(o.deps) + list(o.force):
                    if d.eng == eng_name and not d.dma and d not in o.force:
                        continue
                    if d.dma:
                        if d.cum > dwaited[d.eng][d.dsem]:
                            engine_obj.wait_ge(dsems[d.eng][d.dsem], d.cum)
                            dwaited[d.eng][d.dsem] = d.cum
                    else:
                        if d.cum > waited[d.eng]:
                            engine_obj.wait_ge(sems[d.eng], d.cum)
                            waited[d.eng] = d.cum
                if o.dma and o.cum > 16:
                    if o.cum - 16 > dwaited[eng_name][o.dsem]:
                        engine_obj.wait_ge(dsems[eng_name][o.dsem], o.cum - 16)
                        dwaited[eng_name][o.dsem] = o.cum - 16
                instr = o.fn(engine_obj)
                if o.dma:
                    instr.then_inc(dsems[eng_name][o.dsem], 16)
                elif o.inc_val:
                    assert instr is not None, "op with dependents returned None"
                    instr.then_inc(sems[o.eng], o.inc_val)

        for eng_name in ENGINES:
            if self.queues[eng_name]:
                getattr(block, eng_name)(lambda e, n=eng_name: run_engine(e, n))



def build_k1():
    T = 256
    NT = HALF // T
    TE = T + 2
    NE, NC, ND = 2 * DI // 128, DI // 128, D // 128
    nc = bass.Bass("TRN2", target_bir_lowering=False)

    xT = nc.dram_tensor("xT", [D, HALF + 2], F32, kind="ExternalInput").ap()
    winT = nc.dram_tensor("winT", [D, 2 * DI], F32, kind="ExternalInput").ap()
    dtwT = nc.dram_tensor("dtwT", [DI, DI], F32, kind="ExternalInput").ap()
    woutT = nc.dram_tensor("woutT", [DI, D], F32, kind="ExternalInput").ap()
    convw = nc.dram_tensor("convw", [128, 36], F32, kind="ExternalInput").ap()
    vecs = nc.dram_tensor("vecs", [128, 48], F32, kind="ExternalInput").ap()
    attn0 = nc.dram_tensor("attn0", [D, HALF], BF16, kind="ExternalOutput").ap()
    Mout = nc.dram_tensor("Mout", [DI, HALF], BF16, kind="ExternalOutput").ap()
    carry = nc.dram_tensor("carry", [128, 12], F32, kind="ExternalOutput").ap()

    p = Prog()
    from contextlib import ExitStack
    with ExitStack() as ctx:
        sb = lambda name, shape, dt: ctx.enter_context(nc.sbuf_tensor(name, shape, dt))
        ps = lambda name, shape: ctx.enter_context(nc.psum_tensor(name, shape, F32))
        w_in = sb("w_in", [128, ND * 2 * DI], BF16)
        w_dt = sb("w_dt", [128, NC * DI], BF16)
        w_out = sb("w_out", [128, NC * D], BF16)
        cw = sb("cw", [128, 36], F32)
        vv = sb("vv", [128, 48], F32)
        xb = sb("xb", [128, 2 * ND * TE], BF16)
        xin = sb("xin", [128, NC * TE], BF16)
        zb = sb("zb", [128, NC * T], BF16)
        xc = sb("xc", [128, NC * T], BF16)
        aexp = sb("aexp", [128, NC * T], BF16)
        cum = sb("cum", [128, NC * T], F32)
        szb = sb("szb", [128, NC * T], BF16)
        yb = sb("yb", [128, NC * T], BF16)
        ao = sb("ao", [128, 2 * ND * T], BF16)
        mo = sb("mo", [128, 2 * NC * T], BF16)
        cpre = sb("cpre", [128, 2 * T], F32)
        dts = sb("dts", [128, 2 * T], F32)
        sp1 = sb("sp1", [128, 2 * T], F32)
        sch = sb("sch", [128, 2 * T], F32)
        yf = sb("yf", [128, T], F32)
        ones = sb("ones", [128, T], F32)
        carr = sb("carr", [128, 12], F32)
        ps_a = ps("ps_a", [128, TE])
        ps_b = ps("ps_b", [128, TE])
        ps_c = ps("ps_c", [128, T])
        ps_d = ps("ps_d", [128, T])
        block = ctx.enter_context(nc.Block())
        psAB = [ps_a, ps_b]
        psCD = [ps_c, ps_d]

        lds = []
        for dch in range(ND):
            lds.append(p.op("gpsimd", lambda g, dch=dch: g.dma_start(
                out=w_in[:, dch * 2 * DI:(dch + 1) * 2 * DI],
                in_=winT[dch * 128:(dch + 1) * 128, :]), dma=True))
        for cch in range(NC):
            lds.append(p.op("gpsimd", lambda g, c=cch: g.dma_start(
                out=w_dt[:, c * DI:(c + 1) * DI],
                in_=dtwT[c * 128:(c + 1) * 128, :]), dma=True))
            lds.append(p.op("gpsimd", lambda g, c=cch: g.dma_start(
                out=w_out[:, c * D:(c + 1) * D],
                in_=woutT[c * 128:(c + 1) * 128, :]), dma=True))
        lds.append(p.op("gpsimd", lambda g: g.dma_start(out=cw[:, :], in_=convw), dma=True))
        lds.append(p.op("gpsimd", lambda g: g.dma_start(out=vv[:, :], in_=vecs), dma=True))
        p.op("vector", lambda v: v.memset(ones[:, :], 1.0))
        p.op("vector", lambda v: v.memset(carr[:, :], 0.0))

        ab_hist = []
        cd_hist = []
        win_mm_last = {}
        st_ops = []
        prev = {}

        for t in range(NT):
            x_load = p.op("gpsimd", lambda g, t=t: g.dma_start(
                out=xb[:, (t % 2) * ND * TE:((t % 2) + 1) * ND * TE].rearrange("p (c l) -> p c l", l=TE),
                in_=xT[:, t * T:t * T + TE].rearrange("(c p) l -> p c l", p=128)),
                deps=[prev.get(("win_mm", t - 2))] + (lds if t == 0 else []), dma=True)

            ev_win = []
            for e in range(NE):
                bank = psAB[len(ab_hist) % 2]
                dep_bank = ab_hist[-2] if len(ab_hist) >= 2 else None

                def mm_fn(tn, t=t, e=e, bank=bank):
                    for k in range(ND):
                        ins = tn.matmul(
                            bank[:, :TE],
                            w_in[:, k * 2 * DI + e * 128: k * 2 * DI + (e + 1) * 128],
                            xb[:, (t % 2) * ND * TE + k * TE:(t % 2) * ND * TE + (k + 1) * TE],
                            start=(k == 0), stop=(k == ND - 1))
                    return ins
                mm = p.op("tensor", mm_fn, deps=[x_load, dep_bank])
                prev[("win_mm", t)] = mm
                if e < NC:
                    ev = p.op("vector", lambda v, e=e, bank=bank: v.tensor_copy(
                        xin[:, e * TE:(e + 1) * TE], bank[:, :TE]), deps=[mm])
                else:
                    ev = p.op("vector", lambda v, e=e, bank=bank: v.tensor_copy(
                        zb[:, (e - NC) * T:(e - NC + 1) * T], bank[:, 1:T + 1]),
                        deps=[mm, prev.get(("sz", t - 1, e - NC))])
                ab_hist.append(ev)
                ev_win.append(ev)

            silu_xc = []
            for c in range(NC):
                cb = cpre[:, (c % 2) * T:((c % 2) + 1) * T]
                dep_war = silu_xc[c - 2] if c >= 2 else prev.get(("sx", t - 1, NC - 2 + c))

                def conv_fn(v, c=c, cb=cb):
                    v.tensor_scalar(cb, xin[:, c * TE + 2: c * TE + 2 + T],
                                    cw[:, 3 * c + 2: 3 * c + 3], None, ALU.mult)
                    v.scalar_tensor_tensor(cb, xin[:, c * TE + 1: c * TE + 1 + T],
                                           cw[:, 3 * c + 1: 3 * c + 2], cb,
                                           ALU.mult, ALU.add)
                    return v.scalar_tensor_tensor(cb, xin[:, c * TE: c * TE + T],
                                                  cw[:, 3 * c: 3 * c + 1], cb,
                                                  ALU.mult, ALU.add)
                cv = p.op("vector", conv_fn, deps=[ev_win[c], dep_war])
                sx = p.op("scalar", lambda s, c=c, cb=cb: s.activation(
                    xc[:, c * T:(c + 1) * T], cb, AF.Silu,
                    bias=vv[:, c:c + 1], scale=1.0),
                    deps=[cv, prev.get(("mmdt_last", t - 1)),
                          prev.get(("y3", t - 1, c))])
                silu_xc.append(sx)
                prev[("sx", t, c)] = sx

            for e in range(NC):
                gi = len(cd_hist)
                bank = psCD[gi % 2]
                dep_bank = cd_hist[-2] if gi >= 2 else None

                def mmdt_fn(tn, e=e, bank=bank):
                    for k in range(NC):
                        ins = tn.matmul(bank[:, :T],
                                        w_dt[:, k * DI + e * 128: k * DI + (e + 1) * 128],
                                        xc[:, k * T:(k + 1) * T],
                                        start=(k == 0), stop=(k == NC - 1))
                    return ins
                mmdt = p.op("tensor", mmdt_fn, deps=[silu_xc[-1], dep_bank])
                prev[("mmdt_last", t)] = mmdt
                d2 = dts[:, (e % 2) * T:((e % 2) + 1) * T]
                s2 = sp1[:, (e % 2) * T:((e % 2) + 1) * T]
                s3 = sch[:, (e % 2) * T:((e % 2) + 1) * T]
                dep_war = prev.get(("y3", t, e - 2)) if e >= 2 else prev.get(("y3", t - 1, NC - 2 + e))
                rl = p.op("scalar", lambda s, e=e, bank=bank, s2=s2: s.activation(
                    s2, bank[:, :T], AF.Relu, bias=vv[:, 12 + e:13 + e], scale=1.0),
                    deps=[mmdt, dep_war])
                ab = p.op("scalar", lambda s, e=e, bank=bank, d2=d2: s.activation(
                    d2, bank[:, :T], AF.Abs, bias=vv[:, 12 + e:13 + e], scale=1.0),
                    deps=[mmdt])
                cd_hist.append(ab)
                p.op("scalar", lambda s, d2=d2: s.activation(d2, d2, AF.Exp, scale=-1.0))
                ln_op = p.op("scalar", lambda s, d2=d2: s.activation(d2, d2, AF.Ln, bias=1.0))
                sp_add = p.op("vector", lambda v, d2=d2, s2=s2: v.tensor_add(d2, s2, d2),
                              deps=[ln_op])
                aex = p.op("scalar", lambda s, e=e, d2=d2: s.activation(
                    aexp[:, e * T:(e + 1) * T], d2, AF.Exp, bias=0.0,
                    scale=vv[:, 24 + e:25 + e]),
                    deps=[sp_add, prev.get(("y3", t - 1, e))])
                sb1 = p.op("vector", lambda v, e=e, d2=d2, s3=s3: v.tensor_mul(
                    s3, xc[:, e * T:(e + 1) * T], d2),
                    deps=[prev.get(("mmwo_last", t - 1))] if e == 0 else [])
                p.op("vector", lambda v, e=e, s3=s3: v.tensor_mul(
                    s3, s3, aexp[:, e * T:(e + 1) * T]), deps=[aex])
                p.op("vector", lambda v, e=e, s3=s3: v.tensor_tensor_scan(
                    cum[:, e * T:(e + 1) * T], ones[:, :], s3,
                    0.0, ALU.mult, ALU.add))
                sz = p.op("scalar", lambda s, e=e: s.activation(
                    szb[:, e * T:(e + 1) * T], zb[:, e * T:(e + 1) * T], AF.Silu),
                    deps=[ev_win[NC + e], prev.get(("y3", t - 1, e))])
                prev[("sz", t, e)] = sz
                mb = p.op("vector", lambda v, e=e, t=t: v.tensor_mul(
                    mo[:, (t % 2) * NC * T + e * T:(t % 2) * NC * T + (e + 1) * T],
                    aexp[:, e * T:(e + 1) * T], szb[:, e * T:(e + 1) * T]),
                    deps=[aex, sz, prev.get(("stm", t - 1))])
                prev[("mb", t, e)] = mb
                p.op("vector", lambda v, e=e: v.scalar_tensor_tensor(
                    yf[:, :], cum[:, e * T:(e + 1) * T], carr[:, e:e + 1],
                    aexp[:, e * T:(e + 1) * T], ALU.add, ALU.mult))
                p.op("vector", lambda v, e=e: v.tensor_add(
                    carr[:, e:e + 1], carr[:, e:e + 1],
                    cum[:, (e + 1) * T - 1:(e + 1) * T]))
                p.op("vector", lambda v, e=e: v.scalar_tensor_tensor(
                    yf[:, :], xc[:, e * T:(e + 1) * T], vv[:, 36 + e:37 + e],
                    yf[:, :], ALU.mult, ALU.add))
                y3 = p.op("vector", lambda v, e=e: v.tensor_mul(
                    yb[:, e * T:(e + 1) * T], yf[:, :], szb[:, e * T:(e + 1) * T]))
                prev[("y3", t, e)] = y3

            ev_wo = []
            for e in range(ND):
                gi = len(cd_hist)
                bank = psCD[gi % 2]
                dep_bank = cd_hist[-2]

                def mmwo_fn(tn, e=e, bank=bank):
                    for k in range(NC):
                        ins = tn.matmul(bank[:, :T],
                                        w_out[:, k * D + e * 128: k * D + (e + 1) * 128],
                                        yb[:, k * T:(k + 1) * T],
                                        start=(k == 0), stop=(k == NC - 1))
                    return ins
                mmwo = p.op("tensor", mmwo_fn, deps=[prev[("y3", t, NC - 1)], dep_bank])
                prev[("mmwo_last", t)] = mmwo
                ev = p.op("vector", lambda v, e=e, bank=bank, t=t: v.tensor_copy(
                    ao[:, (t % 2) * ND * T + e * T:(t % 2) * ND * T + (e + 1) * T],
                    bank[:, :T]), deps=[mmwo, prev.get(("sta", t - 1))])
                cd_hist.append(ev)
                ev_wo.append(ev)

            sta = p.op("sync", lambda s, t=t: s.dma_start(
                out=attn0[:, t * T:(t + 1) * T].rearrange("(c p) l -> p c l", p=128),
                in_=ao[:, (t % 2) * ND * T:((t % 2) + 1) * ND * T].rearrange("p (c l) -> p c l", l=T)),
                deps=ev_wo, dma=True)
            stm = p.op("sync", lambda s, t=t: s.dma_start(
                out=Mout[:, t * T:(t + 1) * T].rearrange("(c p) l -> p c l", p=128),
                in_=mo[:, (t % 2) * NC * T:((t % 2) + 1) * NC * T].rearrange("p (c l) -> p c l", l=T)),
                deps=[prev[("mb", t, e)] for e in range(NC)], dma=True)
            prev[("sta", t)], prev[("stm", t)] = sta, stm
            st_ops += [sta, stm]

        stc = p.op("gpsimd", lambda g: g.dma_start(out=carry, in_=carr[:, :]),
                   deps=[prev[("y3", NT - 1, NC - 1)]], dma=True)
        st_ops.append(stc)
        p.op("gpsimd", lambda g: g.nop(), deps=st_ops)
        p.emit(nc, block, ctx)
    return nc



def _k1_inputs(inputs):
    x = np.asarray(inputs["L0_attn_x"])
    winT = np.ascontiguousarray(np.asarray(inputs["L0_attn_W_in"]).T)
    dtwT = np.ascontiguousarray(np.asarray(inputs["L0_attn_dt_w"]).T)
    woutT = np.ascontiguousarray(np.asarray(inputs["L0_attn_W_out"]).T)
    conv_w = np.asarray(inputs["L0_attn_conv_w"])
    convw = np.ascontiguousarray(
        conv_w[:, 0, :].reshape(12, 128, 3).transpose(1, 0, 2).reshape(128, 36))
    vecs = np.stack([np.asarray(inputs["L0_attn_conv_b"]),
                     np.asarray(inputs["L0_attn_dt_b"]),
                     np.asarray(inputs["L0_attn_A"]),
                     np.asarray(inputs["L0_attn_Dp"])], 0)
    vecs = np.ascontiguousarray(vecs.reshape(4, 12, 128).transpose(2, 0, 1).reshape(128, 48))

    maps = []
    for c in range(NCORES):
        b, h = c >> 1, c & 1
        lo = h * HALF
        xpad = np.zeros((D, HALF + 2), np.float32)
        xs = x[b].T
        lo0, hi0 = lo - 1, lo + HALF + 1
        src_lo, src_hi = max(lo0, 0), min(hi0, L)
        xpad[:, (src_lo - lo0):(src_lo - lo0) + (src_hi - src_lo)] = xs[:, src_lo:src_hi]
        maps.append({"xT": np.ascontiguousarray(xpad), "winT": winT, "dtwT": dtwT,
                     "woutT": woutT, "convw": convw, "vecs": vecs})
    return maps


def run_k1(inputs):
    nc = build_k1()
    maps = _k1_inputs(inputs)
    res = run_bass_kernel_spmd(nc, maps, core_ids=list(range(NCORES)))
    return res.results





T2 = 512
NDC = D // 128


def emit_ln(p, nc, bufs, t_dep_ops, tch, eps=1e-5, src16=False):
    sq, tb = bufs["sq"], bufs["tb"]
    ps_s, ps_q = bufs["ps_s"], bufs["ps_q"]
    ps_e, ps_f = bufs["ps_e"], bufs["ps_f"]
    rows, cst = bufs["rows"], bufs["cst"]
    abc, bbc = bufs["abc"], bufs["bbc"]
    war = bufs.get("ln_war")

    cast_ops = []
    mm_s = mm_q = None
    for i in range(NDC):
        if src16:
            tsrc = tch(i)
            cb = t_dep_ops[i]
        else:
            cb = p.op("vector", lambda v, i=i: v.tensor_copy(
                tb[:, (i % 2) * T2:((i % 2) + 1) * T2], tch(i)),
                deps=[t_dep_ops[i]] + ([war] if war and i < 2 else [])
                + ([cast_ops[i - 2]] if i >= 2 else []))
            tsrc = tb[:, (i % 2) * T2:((i % 2) + 1) * T2]
        sqo = p.op("scalar", lambda s, i=i, tsrc=tsrc: s.activation(
            sq[:, 0:T2], tsrc, AF.Square),
            deps=[cb] + ([cast_ops[i - 1]] if i >= 1 else [])
            + ([war] if (war and src16 and i < 2) else []))
        mm_s = p.op("tensor", lambda tn, i=i, tsrc=tsrc: tn.matmul(
            ps_s[0:1, :T2], cst[:, 0:1], tsrc,
            start=(i == 0), stop=(i == NDC - 1)), deps=[cb, war if i < 1 else None])
        mm_q = p.op("tensor", lambda tn, i=i: tn.matmul(
            ps_q[0:1, :T2], cst[:, 0:1], sq[:, 0:T2],
            start=(i == 0), stop=(i == NDC - 1)), deps=[sqo])
        cast_ops.append(mm_q)

    m_ = rows[0:1, 0:T2]
    q_ = rows[0:1, T2:2 * T2]
    v_ = rows[0:1, 2 * T2:3 * T2]
    r1 = p.op("vector", lambda v: v.tensor_scalar(m_, ps_s[0:1, :T2], 1.0 / D, None, ALU.mult),
              deps=[mm_s])
    r2 = p.op("vector", lambda v: v.tensor_scalar(q_, ps_q[0:1, :T2], 1.0 / D, None, ALU.mult),
              deps=[mm_q])
    r3 = p.op("vector", lambda v: v.tensor_mul(v_, m_, m_))
    r4 = p.op("vector", lambda v: v.tensor_sub(v_, q_, v_))
    if "eps_op" not in bufs:
        bufs["eps_op"] = p.op("vector", lambda v: v.memset(bufs["epsc"][:, :], eps))
    r5 = p.op("scalar", lambda s: s.activation(v_, v_, AF.Sqrt, bias=bufs["epsc"][0:1, 0:1]),
              deps=[r4, bufs["eps_op"]])
    r6 = p.op("vector", lambda v: v.reciprocal(v_, v_), deps=[r5])
    ab16 = bufs["rows16"][0:1, 0:T2]
    mb16 = bufs["rows16"][0:1, T2:2 * T2]
    r7 = p.op("vector", lambda v: v.tensor_copy(ab16, v_))
    r8 = p.op("vector", lambda v: v.tensor_mul(m_, m_, v_))
    r9 = p.op("vector", lambda v: v.tensor_copy(mb16, m_))
    ba = p.op("tensor", lambda tn: tn.matmul(ps_e[:, :T2], cst[0:1, 128:256], ab16,
                                             start=True, stop=True), deps=[r7, war])
    bb = p.op("tensor", lambda tn: tn.matmul(ps_f[:, :T2], cst[0:1, 256:384], mb16,
                                             start=True, stop=True), deps=[r9])
    ea = p.op("vector", lambda v: v.tensor_copy(abc[:, :], ps_e[:, :T2]), deps=[ba])
    eb = p.op("vector", lambda v: v.tensor_copy(bbc[:, :], ps_f[:, :T2]), deps=[bb])
    return [ea, eb]


def ln_bufs(nc, ctx, tag, need_tb=True):
    sb = lambda name, shape, dt: ctx.enter_context(nc.sbuf_tensor(name + tag, shape, dt))
    ps = lambda name, shape: ctx.enter_context(nc.psum_tensor(name + tag, shape, F32))
    return dict(
        sq=sb("ln_sq", [128, T2], BF16),
        tb=sb("ln_tb", [128, 2 * T2], BF16) if need_tb else None,
        rows=sb("ln_rows", [1, 3 * T2], F32),
        rows16=sb("ln_rows16", [1, 2 * T2], BF16),
        cst=None,
        abc=sb("ln_abc", [128, T2], BF16),
        bbc=sb("ln_bbc", [128, T2], BF16),
        epsc=sb("ln_eps", [1, 1], F32),
        ps_s=ps("ln_ps_s", [1, T2]),
        ps_q=ps("ln_ps_q", [1, T2]),
        ps_e=ps("ln_ps_e", [128, T2]),
        ps_f=ps("ln_ps_f", [128, T2]),
    )


def emit_cst_init(p, cst):
    o1 = p.op("vector", lambda v: v.memset(cst[:, 0:1], 1.0))
    o2 = p.op("vector", lambda v: v.memset(cst[0:1, 128:256], 1.0))
    o3 = p.op("vector", lambda v: v.memset(cst[0:1, 256:384], -1.0))
    return [o1, o2, o3]



def build_mixout(kch, with_corr):
    NK = kch
    NT = HALF // T2
    nc = bass.Bass("TRN2", target_bir_lowering=False)
    wT = nc.dram_tensor("wT", [NK * 128, D], F32, kind="ExternalInput").ap()
    rhs_h = nc.dram_tensor("rhs", [NK * 128, HALF], BF16, kind="ExternalInput").ap()
    if with_corr:
        carryP = nc.dram_tensor("carryP", [128, NK], F32, kind="ExternalInput").ap()
        extra = nc.dram_tensor("extra", [D, HALF], BF16, kind="ExternalInput").ap()
        xs = nc.dram_tensor("xs", [D, HALF], F32, kind="ExternalInput").ap()
    else:
        xs = nc.dram_tensor("xs", [D, HALF], BF16, kind="ExternalInput").ap()
    x1o = nc.dram_tensor("x1o", [D, HALF], BF16, kind="ExternalOutput").ap()

    p = Prog()
    from contextlib import ExitStack
    with ExitStack() as ctx:
        sb = lambda name, shape, dt: ctx.enter_context(nc.sbuf_tensor(name, shape, dt))
        psf = lambda name, shape: ctx.enter_context(nc.psum_tensor(name, shape, F32))
        w_ = sb("w_", [128, NK * D], BF16)
        rb = sb("rb", [128, 2 * NK * T2], BF16)
        xb_ = sb("xb_", [128, 2 * NDC * T2], F32 if with_corr else BF16)
        ex_ = sb("ex_", [128, 2 * NDC * T2], BF16) if with_corr else None
        cp_ = sb("cp_", [128, NK], F32) if with_corr else None
        mc_ = sb("mc_", [128, NK * T2], BF16) if with_corr else None
        tf = sb("tf", [128, NDC * T2], F32)
        x1t = sb("x1t", [128, 2 * NDC * T2], BF16)
        cst = sb("cst", [128, 384], BF16)
        u_ = sb("u_", [128, T2], F32)
        lb = ln_bufs(nc, ctx, "")
        lb["cst"] = cst
        ps_a = psf("ps_a", [128, T2])
        ps_b = psf("ps_b", [128, T2])
        block = ctx.enter_context(nc.Block())

        lds = []
        for k in range(NK):
            lds.append(p.op("gpsimd", lambda g, k=k: g.dma_start(
                out=w_[:, k * D:(k + 1) * D], in_=wT[k * 128:(k + 1) * 128, :]), dma=True))
        if with_corr:
            lds.append(p.op("gpsimd", lambda g: g.dma_start(out=cp_[:, :], in_=carryP), dma=True))
        cst_ops = emit_cst_init(p, cst)
        psAB = [ps_a, ps_b]
        hist = []
        prev = {}

        for t in range(NT):
            bi = t % 2
            ld_r = p.op("gpsimd", lambda g, t=t, bi=bi: g.dma_start(
                out=rb[:, bi * NK * T2:(bi + 1) * NK * T2].rearrange("p (c l) -> p c l", l=T2),
                in_=rhs_h[:, t * T2:(t + 1) * T2].rearrange("(c p) l -> p c l", p=128)),
                deps=[prev.get(("mmlast", t - 2))] + (lds if t == 0 else []), dma=True)
            ld_x = p.op("sync", lambda s, t=t, bi=bi: s.dma_start(
                out=xb_[:, bi * NDC * T2:(bi + 1) * NDC * T2].rearrange("p (c l) -> p c l", l=T2),
                in_=xs[:, t * T2:(t + 1) * T2].rearrange("(c p) l -> p c l", p=128)),
                deps=[prev.get(("tadd", t - 2))], dma=True)
            ld_e = None
            if with_corr:
                ld_e = p.op("sync", lambda s, t=t, bi=bi: s.dma_start(
                    out=ex_[:, bi * NDC * T2:(bi + 1) * NDC * T2].rearrange("p (c l) -> p c l", l=T2),
                    in_=extra[:, t * T2:(t + 1) * T2].rearrange("(c p) l -> p c l", p=128)),
                    deps=[prev.get(("tadd", t - 2))], dma=True)
                mc_ops = []
                for k in range(NK):
                    mo_ = p.op("vector", lambda v, k=k, bi=bi: v.tensor_scalar(
                        mc_[:, k * T2:(k + 1) * T2],
                        rb[:, bi * NK * T2 + k * T2: bi * NK * T2 + (k + 1) * T2],
                        cp_[:, k:k + 1], None, ALU.mult),
                        deps=[ld_r] + ([prev.get(("mmlast", t - 1))] if k < 1 else []))
                    mc_ops.append(mo_)

            tadds = []
            for e in range(NDC):
                gi = len(hist)
                bank = psAB[gi % 2]
                dep_bank = hist[-2] if gi >= 2 else None

                def mmfn(tn, e=e, bank=bank, bi=bi, t=t):
                    for k in range(NK):
                        if with_corr:
                            r = mc_[:, k * T2:(k + 1) * T2]
                        else:
                            r = rb[:, bi * NK * T2 + k * T2: bi * NK * T2 + (k + 1) * T2]
                        ins = tn.matmul(bank[:, :T2],
                                        w_[:, k * D + e * 128: k * D + (e + 1) * 128],
                                        r, start=(k == 0), stop=(k == NK - 1))
                    return ins
                mm = p.op("tensor", mmfn,
                          deps=[mc_ops[-1] if with_corr else ld_r, dep_bank])
                prev[("mmlast", t)] = mm
                if with_corr:
                    a1 = p.op("vector", lambda v, e=e, bank=bank, bi=bi: v.tensor_add(
                        u_[:, :], bank[:, :T2],
                        xb_[:, bi * NDC * T2 + e * T2: bi * NDC * T2 + (e + 1) * T2]),
                        deps=[mm, ld_x])
                    a2 = p.op("vector", lambda v, e=e, bi=bi: v.tensor_add(
                        tf[:, e * T2:(e + 1) * T2], u_[:, :],
                        ex_[:, bi * NDC * T2 + e * T2: bi * NDC * T2 + (e + 1) * T2]),
                        deps=[ld_e])
                else:
                    a2 = p.op("vector", lambda v, e=e, bank=bank, bi=bi: v.tensor_add(
                        tf[:, e * T2:(e + 1) * T2], bank[:, :T2],
                        xb_[:, bi * NDC * T2 + e * T2: bi * NDC * T2 + (e + 1) * T2]),
                        deps=[mm, ld_x])
                hist.append(a2)
                tadds.append(a2)
            prev[("tadd", t)] = tadds[-1]

            lb["ln_war"] = prev.get(("apply", t - 1))
            ln_done = emit_ln(p, nc, lb, tadds, lambda i: tf[:, i * T2:(i + 1) * T2])
            applies = []
            for e in range(NDC):
                ap1 = p.op("vector", lambda v, e=e: v.tensor_mul(
                    u_[:, :], tf[:, e * T2:(e + 1) * T2], lb["abc"][:, :]),
                    deps=ln_done if e == 0 else [])
                ap2 = p.op("vector", lambda v, e=e, bi=bi, t=t: v.tensor_add(
                    x1t[:, bi * NDC * T2 + e * T2: bi * NDC * T2 + (e + 1) * T2],
                    u_[:, :], lb["bbc"][:, :]),
                    deps=[prev.get(("st", t - 1))] if e == 0 else [])
                applies.append(ap2)
            prev[("apply", t)] = applies[-1]
            st = p.op("sync", lambda s, t=t, bi=bi: s.dma_start(
                out=x1o[:, t * T2:(t + 1) * T2].rearrange("(c p) l -> p c l", p=128),
                in_=x1t[:, bi * NDC * T2:(bi + 1) * NDC * T2].rearrange("p (c l) -> p c l", l=T2)),
                deps=applies, dma=True)
            prev[("st", t)] = st

        p.op("gpsimd", lambda g: g.nop(), deps=[prev[("st", tt)] for tt in range(NT)])
        p.emit(nc, block, ctx)
    return nc


def build_ffn(final, debug=False):
    NT = HALF // T2
    NF = 4096 // 128
    nc = bass.Bass("TRN2", target_bir_lowering=False)
    x1h = nc.dram_tensor("x1h", [D, HALF], BF16, kind="ExternalInput").ap()
    q1 = nc.dram_tensor("q1", [D, 4096], I32, kind="ExternalInput").ap()
    s1 = nc.dram_tensor("s1", [8, 4096], F32, kind="ExternalInput").ap()
    b1v = nc.dram_tensor("b1v", [128, 32], F32, kind="ExternalInput").ap()
    q2 = nc.dram_tensor("q2", [4096, D], I32, kind="ExternalInput").ap()
    s2 = nc.dram_tensor("s2", [32, D], F32, kind="ExternalInput").ap()
    b2v = nc.dram_tensor("b2v", [128, 8], F32, kind="ExternalInput").ap()
    lngb = nc.dram_tensor("lngb", [128, 16], F32, kind="ExternalInput").ap()
    if final:
        ident = nc.dram_tensor("ident", [128, 128], F32, kind="ExternalInput").ap()
        osh = nc.dram_tensor("osh", [HALF, D], F32, kind="ExternalOutput").ap()
    else:
        x2o = nc.dram_tensor("x2o", [D, HALF], BF16, kind="ExternalOutput").ap()
    if debug:
        x1dbg = nc.dram_tensor("x1dbg", [D, T2], BF16, kind="ExternalOutput").ap()
        abcdbg = nc.dram_tensor("abcdbg", [128, T2], BF16, kind="ExternalOutput").ap()
        hdbg = nc.dram_tensor("hdbg", [D, T2], BF16, kind="ExternalOutput").ap()
        ydbg = nc.dram_tensor("ydbg", [4096, T2], BF16, kind="ExternalOutput").ap()
        tdbg = nc.dram_tensor("tdbg", [D, T2], BF16, kind="ExternalOutput").ap()

    p = Prog()
    from contextlib import ExitStack
    with ExitStack() as ctx:
        sb = lambda name, shape, dt: ctx.enter_context(nc.sbuf_tensor(name, shape, dt))
        psf = lambda name, shape: ctx.enter_context(nc.psum_tensor(name, shape, F32))
        w1 = sb("w1", [128, 8 * 4096], BF16)
        w2 = sb("w2", [128, NF * D], BF16)
        x1t = sb("x1t", [128, 2 * NDC * T2], BF16)
        hb = sb("hb", [128, NDC * T2], BF16)
        yt = sb("yt", [128, NF * T2], BF16)
        t2b = hb
        scb = sb("scb", [128, 512], BF16)
        srow = sb("srow", [1, 512], F32)
        srow16 = sb("srow16", [1, 512], BF16)
        gb = sb("gb", [128, 16], F32)
        b1b = sb("b1b", [128, 32], F32)
        b2b = sb("b2b", [128, 8], F32)
        b2h = sb("b2h", [128, 8], F32)
        cst = sb("cst", [128, 384], BF16)
        u_ = sb("u_", [128, T2], F32)
        u2_ = sb("u2_", [128, T2], BF16)
        lb = ln_bufs(nc, ctx, "", need_tb=False)
        lb["cst"] = cst
        ps_a = psf("ps_a", [128, T2])
        ps_b = psf("ps_b", [128, T2])
        if final:
            idt = sb("idt", [128, 128], F32)
            obuf = sb("obuf", [128, D], F32)
            ps_t = psf("ps_t", [128, 128])
            ps_u = psf("ps_u", [128, 128])
        block = ctx.enter_context(nc.Block())

        psAB = [ps_a, ps_b]
        cst_ops = emit_cst_init(p, cst)
        lds = [p.op("sync", lambda s: s.dma_start(out=gb[:, :], in_=lngb), dma=True),
               p.op("sync", lambda s: s.dma_start(out=b1b[:, :], in_=b1v), dma=True),
               p.op("sync", lambda s: s.dma_start(out=b2b[:, :], in_=b2v), dma=True)]
        b2h_op = p.op("vector", lambda v: v.tensor_scalar(
            b2h[:, :], b2b[:, :], 0.5, None, ALU.mult), deps=[lds[2]])
        if final:
            lds.append(p.op("sync", lambda s: s.dma_start(out=idt[:, :], in_=ident), dma=True))

        dq_done = []
        hist_dq = []
        for m, (qsrc, ssrc, wdst, ngrp, nslc) in enumerate(
                ((q1, s1, w1, 8, 8), (q2, s2, w2, 32, 2))):
            for g in range(ngrp):
                qld = p.op("gpsimd", lambda gg, g=g, qsrc=qsrc, wdst=wdst, nslc=nslc: gg.dma_start(
                    out=wdst[:, g * nslc * 512:(g + 1) * nslc * 512],
                    in_=qsrc[g * 128:(g + 1) * 128, :]), dma=True)
                for j in range(nslc):
                    sld = p.op("sync", lambda s, g=g, j=j, ssrc=ssrc: s.dma_start(
                        out=srow[:, :], in_=ssrc[g:g + 1, j * 512:(j + 1) * 512]),
                        deps=[hist_dq[-1] if hist_dq else None], dma=True)
                    sc = p.op("vector", lambda v: v.tensor_copy(srow16[:, :], srow[:, :]),
                              deps=[sld])
                    bc = p.op("tensor", lambda tn: tn.matmul(
                        ps_a[:, :T2], cst[0:1, 128:256], srow16[0:1, :],
                        start=True, stop=True),
                        deps=[sc, cst_ops[1], hist_dq[-2] if len(hist_dq) >= 2 else None])
                    ev = p.op("vector", lambda v: v.tensor_copy(scb[:, :], ps_a[:, :T2]),
                              deps=[bc])
                    dq = p.op("vector", lambda v, g=g, j=j, wdst=wdst, nslc=nslc: v.scalar_tensor_tensor(
                        wdst[:, (g * nslc + j) * 512:(g * nslc + j + 1) * 512],
                        wdst[:, (g * nslc + j) * 512:(g * nslc + j + 1) * 512],
                        -128.0, scb[:, :], ALU.add, ALU.mult), deps=[qld])
                    hist_dq.append(dq)
            dq_done.append(hist_dq[-1])

        prev = {}
        st_all = []
        for t in range(NT):
            bi = t % 2
            ldx = p.op("sync", lambda s, t=t, bi=bi: s.dma_start(
                out=x1t[:, bi * NDC * T2:(bi + 1) * NDC * T2].rearrange("p (c l) -> p c l", l=T2),
                in_=x1h[:, t * T2:(t + 1) * T2].rearrange("(c p) l -> p c l", p=128)),
                deps=[prev.get(("t2", t - 2, NDC - 1)), prev.get(("sto", t - 2)),
                      prev.get(("oapply", t - 2))], dma=True)
            xch = lambda i, bi=bi: x1t[:, bi * NDC * T2 + i * T2: bi * NDC * T2 + (i + 1) * T2]

            lb["ln_war"] = prev.get(("oapply", t - 1))
            ln1 = emit_ln(p, nc, lb, [ldx] * NDC, xch, src16=True)
            happly = []
            for i in range(NDC):
                h1 = p.op("vector", lambda v, i=i, xch=xch: v.tensor_mul(
                    u_[:, :], xch(i), lb["abc"][:, :]), deps=ln1 if i == 0 else [])
                h2 = p.op("vector", lambda v, i=i: v.tensor_add(u2_[:, :], u_[:, :], lb["bbc"][:, :]))
                h3 = p.op("vector", lambda v, i=i: v.tensor_scalar(
                    hb[:, i * T2:(i + 1) * T2], u2_[:, :], gb[:, i:i + 1],
                    gb[:, 8 + i:9 + i], ALU.mult, ALU.add),
                    deps=[prev.get(("fc1", t - 1, NF - 1)), lds[0],
                          prev.get(("dbg_t",))] if i == 0 else [])
                happly.append(h3)
            prev[("happly", t)] = happly[-1]

            for f in range(NF):
                bank = psAB[f % 2]
                dep_bank = prev.get(("sil", t, f - 2)) or prev.get(("dq0",)) 
                if t == 0 and f < 2:
                    dep_bank = dq_done[1]
                elif f < 2:
                    dep_bank = prev.get(("t2", t - 1, NDC - 1))

                def fc1fn(tn, f=f, bank=bank):
                    for k in range(NDC):
                        ins = tn.matmul(bank[:, :T2],
                                        w1[:, k * 4096 + f * 128: k * 4096 + (f + 1) * 128],
                                        hb[:, k * T2:(k + 1) * T2],
                                        start=(k == 0), stop=(k == NDC - 1))
                    return ins
                mm1 = p.op("tensor", fc1fn, deps=[happly[-1], dep_bank])
                prev[("fc1", t, f)] = mm1
                sil = p.op("scalar", lambda s, f=f, bank=bank: s.activation(
                    yt[:, f * T2:(f + 1) * T2], bank[:, :T2], AF.Silu,
                    bias=b1b[:, f:f + 1], scale=1.0),
                    deps=[mm1] + ([prev.get(("fc2", t - 1, NDC - 1)), lds[1],
                                   prev.get(("dbg_y",))] if f == 0 else []))
                prev[("sil", t, f)] = sil

            if debug and t == 2:
                prev[("dbg_x1",)] = p.op("sync", lambda s, bi=bi: s.dma_start(
                    out=x1dbg.rearrange("(c p) l -> p c l", p=128),
                    in_=x1t[:, bi * NDC * T2:(bi + 1) * NDC * T2].rearrange("p (c l) -> p c l", l=T2)),
                    deps=[happly[-1]], dma=True)
                prev[("dbg_abc",)] = p.op("sync", lambda s: s.dma_start(
                    out=abcdbg, in_=lb["abc"][:, :]), deps=[happly[-1]], dma=True)
                prev[("dbg_h",)] = p.op("sync", lambda s: s.dma_start(
                    out=hdbg.rearrange("(c p) l -> p c l", p=128),
                    in_=hb[:, :NDC * T2].rearrange("p (c l) -> p c l", l=T2)),
                    deps=[happly[-1]], dma=True)
            for e in range(NDC):
                bank = psAB[e % 2]
                dep_bank = prev.get(("t2", t, e - 2)) if e >= 2 else prev.get(("sil", t, NF - 1))

                def fc2fn(tn, e=e, bank=bank):
                    for k in range(NF):
                        ins = tn.matmul(bank[:, :T2],
                                        w2[:, k * D + e * 128: k * D + (e + 1) * 128],
                                        yt[:, k * T2:(k + 1) * T2],
                                        start=(k == 0), stop=(k == NF - 1))
                    return ins
                mm2 = p.op("tensor", fc2fn, deps=[prev[("sil", t, NF - 1)], dep_bank])
                prev[("fc2", t, e)] = mm2
                tc = p.op("vector", lambda v, e=e, bank=bank: v.tensor_scalar(
                    u_[:, :], bank[:, :T2], 0.5, b2h[:, e:e + 1], ALU.mult, ALU.add),
                    deps=[mm2, b2h_op])
                t2o = p.op("vector", lambda v, e=e, xch=xch: v.scalar_tensor_tensor(
                    t2b[:, e * T2:(e + 1) * T2], xch(e), 2.0, u_[:, :],
                    ALU.mult, ALU.add), deps=[tc, prev.get(("dbg_h",))])
                if debug and t == 2 and e == 0:
                    pass
                prev[("t2", t, e)] = t2o

            if debug and t == 2:
                prev[("dbg_y",)] = p.op("sync", lambda s: s.dma_start(
                    out=ydbg.rearrange("(c p) l -> p c l", p=128),
                    in_=yt[:, :].rearrange("p (c l) -> p c l", l=T2)),
                    deps=[prev[("sil", t, NF - 1)], prev[("fc2", t, NDC - 1)]], dma=True)
                prev[("dbg_t",)] = p.op("sync", lambda s: s.dma_start(
                    out=tdbg.rearrange("(c p) l -> p c l", p=128),
                    in_=t2b[:, :NDC * T2].rearrange("p (c l) -> p c l", l=T2)),
                    deps=[prev[("t2", t, NDC - 1)]], dma=True)
            lb["ln_war"] = prev.get(("oapply", t - 1))
            t2ch = lambda i: t2b[:, i * T2:(i + 1) * T2]
            ln2 = emit_ln(p, nc, lb, [prev[("t2", t, i)] for i in range(NDC)], t2ch, src16=True)
            oapply = []
            if not final:
                for i in range(NDC):
                    o1 = p.op("vector", lambda v, i=i: v.tensor_mul(
                        u_[:, :], t2ch(i), lb["abc"][:, :]), deps=ln2 if i == 0 else [])
                    o2 = p.op("vector", lambda v, i=i, bi=bi: v.tensor_add(
                        x1t[:, bi * NDC * T2 + i * T2: bi * NDC * T2 + (i + 1) * T2],
                        u_[:, :], lb["bbc"][:, :]),
                        deps=[prev.get(("sto", t - 1)), prev.get(("dbg_x1",))] if i == 0 else [])
                    oapply.append(o2)
                sto = p.op("sync", lambda s, t=t, bi=bi: s.dma_start(
                    out=x2o[:, t * T2:(t + 1) * T2].rearrange("(c p) l -> p c l", p=128),
                    in_=x1t[:, bi * NDC * T2:(bi + 1) * NDC * T2].rearrange("p (c l) -> p c l", l=T2)),
                    deps=oapply, dma=True)
                prev[("sto", t)] = sto
                prev[("oapply", t)] = oapply[-1]
                st_all.append(sto)
            else:
                x4 = sb if False else None
                for j in range(4):
                    jb = 0
                    evs = []
                    for i in range(NDC):
                        o1 = p.op("vector", lambda v, i=i, j=j: v.tensor_mul(
                            u_[:, :], t2ch(i), lb["abc"][:, :]),
                            deps=(ln2 if (i == 0 and j == 0) else [])
                            + ([prev.get(("stj", t, j - 1))] if i == 0 and j >= 1 else [])
                            + ([prev.get(("stj", t - 1, 3))] if i == 0 and j == 0 and t > 0 else []))
                        o2 = p.op("vector", lambda v, i=i: v.tensor_add(
                            lb["tb"].bitcast(F32)[:, :T2 // 2] if False else u_[:, :],
                            u_[:, :], lb["bbc"][:, :]))
                        bank = ps_t if (i % 2 == 0) else ps_u
                        tr = p.op("tensor", lambda tn, i=i, j=j, bank=bank: tn.transpose(
                            bank[:, :128], u_[:, j * 128:(j + 1) * 128], idt[:, :]),
                            deps=[o2, lds[-1]] + ([evs[i - 2]] if i >= 2 else []))
                        ev = p.op("vector", lambda v, i=i, bank=bank: v.tensor_copy(
                            obuf[:, i * 128:(i + 1) * 128], bank[:, :128]),
                            deps=[tr])
                        evs.append(ev)
                    stj = p.op("sync", lambda s, t=t, j=j: s.dma_start(
                        out=osh[t * T2 + j * 128: t * T2 + (j + 1) * 128, :],
                        in_=obuf[:, :]), deps=evs, dma=True)
                    prev[("stj", t, j)] = stj
                    st_all.append(stj)
                prev[("oapply", t)] = prev[("stj", t, 3)]

        p.op("gpsimd", lambda g: g.nop(), deps=st_all)
        p.emit(nc, block, ctx)
    return nc


def build_proj():
    NT = HALF // T2
    NE = 3072 // 128
    nc = bass.Bass("TRN2", target_bir_lowering=False)
    x2h = nc.dram_tensor("x2h", [D, HALF], BF16, kind="ExternalInput").ap()
    winT = nc.dram_tensor("winT", [D, 3072], F32, kind="ExternalInput").ap()
    pro = nc.dram_tensor("pro", [3072, HALF], BF16, kind="ExternalOutput").ap()

    p = Prog()
    from contextlib import ExitStack
    with ExitStack() as ctx:
        sb = lambda name, shape, dt: ctx.enter_context(nc.sbuf_tensor(name, shape, dt))
        psf = lambda name, shape: ctx.enter_context(nc.psum_tensor(name, shape, F32))
        w_ = sb("w_", [128, NDC * 3072], BF16)
        xt = sb("xt", [128, 2 * NDC * T2], BF16)
        po = sb("po", [128, 2 * NE * T2], BF16)
        ps_a = psf("ps_a", [128, T2])
        ps_b = psf("ps_b", [128, T2])
        block = ctx.enter_context(nc.Block())
        psAB = [ps_a, ps_b]

        lds = []
        for k in range(NDC):
            lds.append(p.op("gpsimd", lambda g, k=k: g.dma_start(
                out=w_[:, k * 3072:(k + 1) * 3072],
                in_=winT[k * 128:(k + 1) * 128, :]), dma=True))
        hist = []
        prev = {}
        st_all = []
        for t in range(NT):
            bi = t % 2
            ldx = p.op("sync", lambda s, t=t, bi=bi: s.dma_start(
                out=xt[:, bi * NDC * T2:(bi + 1) * NDC * T2].rearrange("p (c l) -> p c l", l=T2),
                in_=x2h[:, t * T2:(t + 1) * T2].rearrange("(c p) l -> p c l", p=128)),
                deps=[prev.get(("mm", t - 2))] + (lds if t == 0 else []), dma=True)
            evs = []
            for e in range(NE):
                gi = len(hist)
                bank = psAB[gi % 2]
                dep_bank = hist[-2] if gi >= 2 else None

                def mmfn(tn, e=e, bank=bank, bi=bi):
                    for k in range(NDC):
                        ins = tn.matmul(bank[:, :T2],
                                        w_[:, k * 3072 + e * 128: k * 3072 + (e + 1) * 128],
                                        xt[:, bi * NDC * T2 + k * T2: bi * NDC * T2 + (k + 1) * T2],
                                        start=(k == 0), stop=(k == NDC - 1))
                    return ins
                mm = p.op("tensor", mmfn, deps=[ldx, dep_bank])
                prev[("mm", t)] = mm
                ev = p.op("vector", lambda v, e=e, bank=bank, bi=bi: v.tensor_copy(
                    po[:, bi * NE * T2 + e * T2: bi * NE * T2 + (e + 1) * T2],
                    bank[:, :T2]), deps=[mm, prev.get(("st", t - 1))])
                hist.append(ev)
                evs.append(ev)
            st = p.op("sync", lambda s, t=t, bi=bi: s.dma_start(
                out=pro[:, t * T2:(t + 1) * T2].rearrange("(c p) l -> p c l", p=128),
                in_=po[:, bi * NE * T2:(bi + 1) * NE * T2].rearrange("p (c l) -> p c l", l=T2)),
                deps=evs, dma=True)
            prev[("st", t)] = st
            st_all.append(st)
        p.op("gpsimd", lambda g: g.nop(), deps=st_all)
        p.emit(nc, block, ctx)
    return nc



def _ffn_maps(inputs, pre, x1_per_core):
    q1 = np.ascontiguousarray(np.asarray(inputs[f"{pre}_fc1_q"], dtype=np.int32))
    s1 = np.ascontiguousarray(np.asarray(inputs[f"{pre}_fc1_scales"], dtype=np.float32))
    b1 = np.asarray(inputs[f"{pre}_fc1_bias"], dtype=np.float32)
    q2 = np.ascontiguousarray(np.asarray(inputs[f"{pre}_fc2_q"], dtype=np.int32))
    s2 = np.ascontiguousarray(np.asarray(inputs[f"{pre}_fc2_scales"], dtype=np.float32))
    b2 = np.asarray(inputs[f"{pre}_fc2_bias"], dtype=np.float32)
    g = np.asarray(inputs[f"{pre}_ln_gamma"], dtype=np.float32)
    bt = np.asarray(inputs[f"{pre}_ln_beta"], dtype=np.float32)
    b1v = np.ascontiguousarray(b1.reshape(32, 128).T)
    b2v = np.ascontiguousarray(b2.reshape(8, 128).T)
    lngb = np.ascontiguousarray(np.concatenate(
        [g.reshape(8, 128).T, bt.reshape(8, 128).T], axis=1))
    maps = []
    for c in range(NCORES):
        maps.append({"x1h": x1_per_core[c], "q1": q1, "s1": s1, "b1v": b1v,
                     "q2": q2, "s2": s2, "b2v": b2v, "lngb": lngb})
    return maps


def build_k3():
    NHH = 64
    nc = bass.Bass("TRN2", target_bir_lowering=False)
    Vh = nc.dram_tensor("Vh", [128, 4, L], BF16, kind="ExternalInput").ap()
    G1h = nc.dram_tensor("G1h", [128, 4, L], BF16, kind="ExternalInput").ap()
    G2h = nc.dram_tensor("G2h", [128, 4, L], BF16, kind="ExternalInput").ap()
    tl = nc.dram_tensor("tl", [NHH, L], F32, kind="ExternalInput").ap()
    fw1 = nc.dram_tensor("fw1", [128, 4], F32, kind="ExternalInput").ap()
    w2t = nc.dram_tensor("w2t", [2, 65, 128], F32, kind="ExternalInput").ap()
    WAh = nc.dram_tensor("WAh", [128, 384], BF16, kind="ExternalInput").ap()
    WCh = nc.dram_tensor("WCh", [128, 384], BF16, kind="ExternalInput").ap()
    GBh = nc.dram_tensor("GBh", [128 * 128, 384], BF16, kind="ExternalInput").ap()
    GCh = nc.dram_tensor("GCh", [128 * 128, 384], BF16, kind="ExternalInput").ap()
    VFh = nc.dram_tensor("VFh", [128, 4, L], BF16, kind="ExternalOutput").ap()
    hspec = nc.dram_tensor("hspec", [256, 32768], BF16).ap()

    p = Prog()
    from contextlib import ExitStack
    with ExitStack() as ctx:
        sb = lambda name, shape, dt: ctx.enter_context(nc.sbuf_tensor(name, shape, dt))
        psf = lambda name, shape: ctx.enter_context(nc.psum_tensor(name, shape, F32))
        Zb = sb("Zb", [128, 64 * 4 * 128], BF16)
        Xb = sb("Xb", [128, 64 * 4 * 128], BF16)
        Mb = sb("Mb", [128, 32 * 4 * 128], BF16)
        Ub = sb("Ub", [128, 32 * 4 * 128], BF16)
        Gg = sb("Gg", [128, 4 * 4 * 128], BF16)
        wa = sb("wa", [128, 384], BF16)
        wc = sb("wc", [128, 384], BF16)
        gb_t = sb("gb_t", [128, 2 * 384], BF16)
        gc_t = sb("gc_t", [128, 2 * 384], BF16)
        ht_t = sb("ht_t", [128, 2 * 128], BF16)
        xsc = sb("xsc", [128, 2 * 128], BF16)
        osc = sb("osc", [128, 4 * 64], BF16)
        tlb = sb("tlb", [NHH, 2 * 512], F32)
        w1b = sb("w1b", [128, 4], F32)
        w2b = sb("w2b", [128, 256], BF16)
        hs_st = sb("hs_st", [128, 2 * 256], BF16)
        ps1 = psf("ps1", [128, 128])
        ps2 = psf("ps2", [128, 128])
        ps3 = psf("ps3", [128, 128])
        ps4 = psf("ps4", [128, 128])
        block = ctx.enter_context(nc.Block())

        St = Xb[:, 0:8192]
        HT = Xb[:, 8192:8192 + 16384]

        lds = [p.op("sync", lambda s: s.dma_start(out=wa[:, :], in_=WAh), dma=True),
               p.op("sync", lambda s: s.dma_start(out=wc[:, :], in_=WCh), dma=True),
               p.op("sync", lambda s: s.dma_start(out=w1b[:, :], in_=fw1), dma=True)]
        w2l = p.op("gpsimd", lambda g: g.dma_start(
            out=w2b[0:65, :].rearrange("p (o d) -> p o d", o=2),
            in_=w2t.rearrange("o r d -> r o d")), dma=True)

        prev = {}
        hst_ops = []
        for o in range(2):
            silu_ops = []
            for j in range(16):
                tld = p.op("sync", lambda s, j=j: s.dma_start(
                    out=tlb[:, (j % 2) * 512:(j % 2 + 1) * 512],
                    in_=tl[:, j * 512:(j + 1) * 512]),
                    deps=[silu_ops[j - 2] if j >= 2 else prev.get(("sil_last", o - 1))], dma=True)
                so = p.op("scalar", lambda s, j=j, o=o: s.activation(
                    St[0:NHH, j * 512:(j + 1) * 512],
                    tlb[0:NHH, (j % 2) * 512:(j % 2 + 1) * 512], AF.Silu,
                    bias=w1b[0:NHH, 2 * o + 1:2 * o + 2],
                    scale=w1b[0:NHH, 2 * o:2 * o + 1]),
                    deps=[tld, lds[2]] + ([prev.get(("hmm", o - 1, 127))] if j == 0 else []))
                silu_ops.append(so)
            prev[("sil_last", o)] = silu_ops[-1]
            ones_row = p.op("vector", lambda v: v.memset(St[NHH:NHH + 1, :], 1.0),
                            deps=[prev.get(("hmm", o - 1, 127))])
            for ll in range(128):
                hps = ps1 if ll % 2 == 0 else ps2
                def hmm_fn(tn, ll=ll, o=o, hps=hps):
                    return tn.matmul(hps[0:NHH, :],
                                     St.rearrange("p (lh ll) -> p ll lh", ll=128)[0:65, ll, :],
                                     w2b[0:65, o * 128:(o + 1) * 128],
                                     start=True, stop=True)
                hm = p.op("tensor", hmm_fn,
                          deps=[silu_ops[-1], ones_row, w2l,
                                prev.get(("hev", o, ll - 2)),
                                prev.get(("hev", o - 1, 127)) if ll < 2 else None])
                prev[("hmm", o, ll)] = hm
                ev = p.op("scalar", lambda s, ll=ll, o=o, hps=hps: s.copy(
                    HT.rearrange("p (ll d) -> p ll d", d=128)[64 * o:64 * o + NHH, ll, :],
                    hps[0:NHH, 0:128]), deps=[hm])
                prev[("hev", o, ll)] = ev
        for o in range(2):
            for d in range(128):
                pr_, pi_ = (ps1, ps2) if d % 2 == 0 else (ps3, ps4)
                def hA_fn(tn, o=o, d=d, pr_=pr_, pi_=pi_):
                    st = HT.rearrange("p (ll d) -> p d ll", d=128)[64 * o:64 * o + NHH, d, :]
                    tn.matmul(pr_[:, :], st, wa[64 * o:64 * o + NHH, 0:128], start=True, stop=True)
                    return tn.matmul(pi_[:, :], st, wa[64 * o:64 * o + NHH, 128:256], start=True, stop=True)
                hA = p.op("tensor", hA_fn,
                          deps=[prev[("hev", 1, 127)], prev.get(("zev", o, d - 2)), lds[0],
                                prev.get(("hsev", o - 1, 127)) if d < 2 else None])
                z1 = p.op("scalar", lambda s, o=o, d=d, pr_=pr_: s.copy(
                    Zb.rearrange("p (d pl k) -> p d pl k", pl=2, k=128)[:, d, 0, :], pr_[:, :]),
                    deps=[hA])
                z2 = p.op("scalar", lambda s, o=o, d=d, pi_=pi_: s.copy(
                    Zb.rearrange("p (d pl k) -> p d pl k", pl=2, k=128)[:, d, 1, :], pi_[:, :]),
                    deps=[hA])
                prev[("zev", o, d)] = z2
            for k2 in range(128):
                gbl = p.op("sync", lambda s, k2=k2: s.dma_start(
                    out=gb_t[:, (k2 % 2) * 384:(k2 % 2 + 1) * 384],
                    in_=GBh[k2 * 128:(k2 + 1) * 128, :]),
                    deps=[prev.get(("hB", o, k2 - 2)),
                          prev.get(("gb_last",)) if k2 < 2 else None], dma=True)

                pr_, pi_ = (ps1, ps2) if k2 % 2 == 0 else (ps3, ps4)
                def hB_fn(tn, o=o, k2=k2, pr_=pr_, pi_=pi_):
                    g = gb_t[:, (k2 % 2) * 384:(k2 % 2 + 1) * 384]
                    zre = Zb.rearrange("p (d pl k) -> p pl k d", pl=2, k=128)[:, 0, k2, :]
                    zim = Zb.rearrange("p (d pl k) -> p pl k d", pl=2, k=128)[:, 1, k2, :]
                    tn.matmul(pr_[:, :], g[:, 0:128], zre, start=True, stop=False)
                    tn.matmul(pi_[:, :], g[:, 128:256], zre, start=True, stop=False)
                    tn.matmul(pr_[:, :], g[:, 256:384], zim, start=False, stop=True)
                    return tn.matmul(pi_[:, :], g[:, 0:128], zim, start=False, stop=True)
                hB = p.op("tensor", hB_fn,
                          deps=[prev[("zev", o, 127)], gbl,
                                prev.get(("hsev", o, k2 - 2))])
                prev[("hB", o, k2)] = hB
                if k2 == 127:
                    prev[("gb_last",)] = hB
                hb_off = (k2 % 2) * 256
                e1 = p.op("scalar", lambda s, pr_=pr_, hb_off=hb_off: s.copy(
                    hs_st[:, hb_off:hb_off + 128], pr_[:, :]),
                    deps=[hB, prev.get(("hst", o, k2 - 2)),
                          prev.get(("hst", o - 1, 126 + k2)) if k2 < 2 else None])
                e2 = p.op("scalar", lambda s, pi_=pi_, hb_off=hb_off: s.copy(
                    hs_st[:, hb_off + 128:hb_off + 256], pi_[:, :]), deps=[hB])
                prev[("hsev", o, k2)] = e2
                hst = p.op("gpsimd", lambda g, o=o, k2=k2, hb_off=hb_off: g.dma_start(
                    out=hspec[o * 128 + k2:o * 128 + k2 + 1, :]
                        .rearrange("one (pl k1 d) -> (one k1) pl d", pl=2, d=128),
                    in_=hs_st[:, hb_off:hb_off + 256].rearrange("p (pl d) -> p pl d", pl=2)),
                    deps=[e1, e2], dma=True)
                hst_ops.append(hst)
                prev[("hst", o, k2)] = hst

        def zidx(dd, bp, pl):
            return ((dd * 2 + bp) * 2 + pl) * 128

        vf_sts = []
        for grp in range(2):
            for o in range(2):
                if o == 0:
                    mls = []
                    for b in range(4):
                        for par in range(2):
                            ml = p.op("sync", lambda s, grp=grp, b=b, par=par: s.dma_start(
                                out=Mb.rearrange("p (dh b ll) -> p dh b ll", b=4, ll=128)
                                    [64 * par:64 * par + NHH, :, b, :],
                                in_=Vh[grp * 64:(grp + 1) * 64, b, :]
                                    .rearrange("(dh dp) (lh ll) -> dp lh dh ll", dp=2, ll=128)[par]),
                                deps=(prev.get(("vfst", grp - 1)) or []) + (hst_ops if grp == 0 and b == 0 and par == 0 else []),
                                dma=True)
                            mls.append(ml)
                    src = Mb
                    src_dep = mls
                else:
                    src = Ub
                    src_dep = [prev[("gate", 0, grp, 7)]]
                for dd in range(64):
                    par, dh = dd % 2, dd // 2

                    def A_fn(tn, src=src, par=par, dh=dh, grp=grp):
                        mre = src.rearrange("p (dh b ll) -> p dh b ll", b=4, ll=128)
                        stre = mre[64 * par:64 * par + NHH, dh, 0, :]
                        ins = None
                        for bp in range(2):
                            s_re = mre[64 * par:64 * par + NHH, dh, 2 * bp, :]
                            s_im = mre[64 * par:64 * par + NHH, dh, 2 * bp + 1, :]
                            pr = ps1 if bp == 0 else ps3
                            pi = ps2 if bp == 0 else ps4
                            tn.matmul(pr[:, :], s_re, wa[64 * par:64 * par + NHH, 0:128], start=True, stop=False)
                            tn.matmul(pi[:, :], s_re, wa[64 * par:64 * par + NHH, 128:256], start=True, stop=False)
                            tn.matmul(pr[:, :], s_im, wa[64 * par:64 * par + NHH, 256:384], start=False, stop=True)
                            ins = tn.matmul(pi[:, :], s_im, wa[64 * par:64 * par + NHH, 0:128], start=False, stop=True)
                        return ins
                    Amm = p.op("tensor", A_fn,
                               deps=src_dep + [prev.get(("Aev", o, grp, dd - 1)),
                                               prev.get(("phase_uev",))])
                    prev[("Amm", o, grp, dd)] = Amm
                    evs = []
                    for bp in range(2):
                        pr = ps1 if bp == 0 else ps3
                        pi = ps2 if bp == 0 else ps4
                        e1 = p.op("scalar", lambda s, dd=dd, bp=bp, pr=pr: s.copy(
                            Zb[:, zidx(dd, bp, 0):zidx(dd, bp, 0) + 128], pr[:, :]), deps=[Amm])
                        e2 = p.op("scalar", lambda s, dd=dd, bp=bp, pi=pi: s.copy(
                            Zb[:, zidx(dd, bp, 1):zidx(dd, bp, 1) + 128], pi[:, :]), deps=[Amm])
                        evs += [e1, e2]
                    prev[("Aev", o, grp, dd)] = evs[-1]
                for k2 in range(128):
                    gbl = p.op("sync", lambda s, k2=k2: s.dma_start(
                        out=gb_t[:, (k2 % 2) * 384:(k2 % 2 + 1) * 384],
                        in_=GBh[k2 * 128:(k2 + 1) * 128, :]),
                        deps=[prev.get(("Bmm", o, grp, k2 - 2)),
                              prev.get(("gb_last",)) if k2 < 2 else None], dma=True)
                    htl = p.op("gpsimd", lambda g, k2=k2, o=o, grp=grp: g.dma_start(
                        out=ht_t[:, (k2 % 2) * 128:(k2 % 2 + 1) * 128]
                            .rearrange("p (pl d) -> p pl d", pl=2),
                        in_=hspec[o * 128 + k2:o * 128 + k2 + 1, :]
                            .rearrange("one (pl k1 d) -> (one k1) pl d", pl=2, d=128)[:, :, grp * 64:(grp + 1) * 64]),
                        deps=[prev.get(("hprod", o, grp, k2 - 2)), prev[("hst", o, k2)],
                              prev.get(("ht_last",)) if k2 < 2 else None], dma=True)

                    pr_, pi_ = (ps1, ps2) if k2 % 2 == 0 else (ps3, ps4)
                    def B_fn(tn, k2=k2, pr_=pr_, pi_=pi_):
                        g = gb_t[:, (k2 % 2) * 384:(k2 % 2 + 1) * 384]
                        zr = Zb.rearrange("p (dd bp pl k) -> p pl k bp dd", bp=2, pl=2, k=128)
                        zre = zr[:, 0, k2, :, :]
                        zim = zr[:, 1, k2, :, :]
                        tn.matmul(pr_[:, :], g[:, 0:128], zre, start=True, stop=False)
                        tn.matmul(pi_[:, :], g[:, 128:256], zre, start=True, stop=False)
                        tn.matmul(pr_[:, :], g[:, 256:384], zim, start=False, stop=True)
                        return tn.matmul(pi_[:, :], g[:, 0:128], zim, start=False, stop=True)
                    Bmm = p.op("tensor", B_fn,
                               deps=[prev[("Aev", o, grp, 63)], gbl,
                                     prev.get(("xev", o, grp, k2 - 2))])
                    prev[("Bmm", o, grp, k2)] = Bmm
                    if k2 == 127:
                        prev[("gb_last",)] = Bmm
                    xe1 = p.op("scalar", lambda s, pr_=pr_: s.copy(xsc[:, 0:128], pr_[:, :]),
                               deps=[Bmm, prev.get(("vlast", o, grp, k2 - 1)),
                                     prev.get(("hprod", o, grp, k2 - 1))])
                    xe2 = p.op("scalar", lambda s, pi_=pi_: s.copy(xsc[:, 128:256], pi_[:, :]), deps=[Bmm])
                    prev[("xev", o, grp, k2)] = xe2
                    pops = []
                    for bp in range(2):
                        xr = xsc[:, bp * 64:bp * 64 + 64]
                        xi = xsc[:, 128 + bp * 64:128 + bp * 64 + 64]
                        hr = ht_t[:, (k2 % 2) * 128:(k2 % 2) * 128 + 64]
                        hi = ht_t[:, (k2 % 2) * 128 + 64:(k2 % 2) * 128 + 128]
                        xre_dst = Xb.rearrange("p (dd bp pl k) -> p bp pl k dd", bp=2, pl=2, k=128)[:, bp, 0, k2, :]
                        xim_dst = Xb.rearrange("p (dd bp pl k) -> p bp pl k dd", bp=2, pl=2, k=128)[:, bp, 1, k2, :]
                        v1 = p.op("vector", lambda v, xr=xr, hr=hr, bp=bp: v.tensor_mul(
                            osc[:, bp * 64:bp * 64 + 64], xr, hr),
                            deps=[xe1, xe2, htl, prev.get(("hprod", o, grp, k2 - 1)),
                                  prev.get(("Apmm", o, grp, 63)) if k2 < 2 else None])
                        v2 = p.op("vector", lambda v, xi=xi, hi=hi, bp=bp: v.tensor_mul(
                            osc[:, 128 + bp * 64:128 + bp * 64 + 64], xi, hi))
                        v3 = p.op("vector", lambda v, bp=bp, xre_dst=xre_dst: v.tensor_sub(
                            xre_dst, osc[:, bp * 64:bp * 64 + 64], osc[:, 128 + bp * 64:128 + bp * 64 + 64]),
                            force=[v1, v2])
                        g1_ = p.op("gpsimd", lambda g, xr=xr, hi=hi, bp=bp: g.tensor_mul(
                            osc[:, 256 + bp * 32:256 + bp * 32 + 32] if False else osc[:, bp * 64:bp * 64 + 64],
                            xr, hi), deps=[v3])
                        g2_ = p.op("gpsimd", lambda g, xi=xi, hr=hr, bp=bp: g.tensor_mul(
                            osc[:, 128 + bp * 64:128 + bp * 64 + 64], xi, hr), deps=[v3])
                        g3_ = p.op("gpsimd", lambda g, bp=bp, xim_dst=xim_dst: g.tensor_add(
                            xim_dst, osc[:, bp * 64:bp * 64 + 64], osc[:, 128 + bp * 64:128 + bp * 64 + 64]),
                            force=[g1_, g2_])
                        pops.append(g3_)
                        if bp == 1:
                            prev[("vlast", o, grp, k2)] = v3
                    prev[("hprod", o, grp, k2)] = pops[-1]
                    if k2 == 127:
                        prev[("ht_last",)] = pops[-1]
                for dd in range(64):
                    def Ap_fn(tn, dd=dd):
                        xv = Xb.rearrange("p (dd bp pl k) -> p dd bp pl k", bp=2, pl=2, k=128)
                        ins = None
                        for bp in range(2):
                            sre = xv[:, dd, bp, 0, :]
                            sim = xv[:, dd, bp, 1, :]
                            pr = ps1 if bp == 0 else ps3
                            pi = ps2 if bp == 0 else ps4
                            tn.matmul(pr[:, :], sre, wc[:, 0:128], start=True, stop=False)
                            tn.matmul(pi[:, :], sre, wc[:, 128:256], start=True, stop=False)
                            tn.matmul(pr[:, :], sim, wc[:, 256:384], start=False, stop=True)
                            ins = tn.matmul(pi[:, :], sim, wc[:, 0:128], start=False, stop=True)
                        return ins
                    Apmm = p.op("tensor", Ap_fn,
                                deps=[prev[("hprod", o, grp, 127)],
                                      prev.get(("Apev", o, grp, dd - 1)),
                                      prev[("xev", o, grp, 127)], lds[1]])
                    prev[("Apmm", o, grp, dd)] = Apmm
                    for bp in range(2):
                        pr = ps1 if bp == 0 else ps3
                        pi = ps2 if bp == 0 else ps4
                        e1 = p.op("scalar", lambda s, dd=dd, bp=bp, pr=pr: s.copy(
                            Zb[:, zidx(dd, bp, 0):zidx(dd, bp, 0) + 128], pr[:, :]), deps=[Apmm])
                        e2 = p.op("scalar", lambda s, dd=dd, bp=bp, pi=pi: s.copy(
                            Zb[:, zidx(dd, bp, 1):zidx(dd, bp, 1) + 128], pi[:, :]), deps=[Apmm])
                    prev[("Apev", o, grp, dd)] = e2
                gsrc = G1h if o == 0 else G2h
                udst = Ub if o == 0 else Mb
                for m1 in range(128):
                    gcl = p.op("sync", lambda s, m1=m1: s.dma_start(
                        out=gc_t[:, (m1 % 2) * 384:(m1 % 2 + 1) * 384],
                        in_=GCh[m1 * 128:(m1 + 1) * 128, :]),
                        deps=[prev.get(("Bpmm", o, grp, m1 - 2)),
                              prev.get(("gc_last",)) if m1 < 2 else None], dma=True)

                    pr_, pi_ = (ps1, ps2) if m1 % 2 == 0 else (ps3, ps4)
                    def Bp_fn(tn, m1=m1, pr_=pr_, pi_=pi_):
                        g = gc_t[:, (m1 % 2) * 384:(m1 % 2 + 1) * 384]
                        pv = Zb.rearrange("p (dd bp pl k) -> p pl k bp dd", bp=2, pl=2, k=128)
                        pre = pv[:, 0, m1, :, :]
                        pim = pv[:, 1, m1, :, :]
                        tn.matmul(pr_[:, :], g[:, 0:128], pre, start=True, stop=False)
                        tn.matmul(pi_[:, :], g[:, 128:256], pre, start=True, stop=False)
                        tn.matmul(pr_[:, :], g[:, 256:384], pim, start=False, stop=True)
                        return tn.matmul(pi_[:, :], g[:, 0:128], pim, start=False, stop=True)
                    Bpmm = p.op("tensor", Bp_fn,
                                deps=[prev[("Apev", o, grp, 63)], gcl,
                                      prev.get(("uev", o, grp, m1 - 2))])
                    prev[("Bpmm", o, grp, m1)] = Bpmm
                    if m1 == 127:
                        prev[("gc_last",)] = Bpmm
                    last = None
                    for pl in range(2):
                        pp = pr_ if pl == 0 else pi_
                        for par in range(2):
                            for bp in range(2):
                                last = p.op("vector",
                                            lambda v, pp=pp, par=par, bp=bp, pl=pl, m1=m1, udst=udst: v.tensor_copy(
                                    udst.rearrange("p (dh b ll) -> p dh b ll", b=4, ll=128)
                                        [64 * par:64 * par + NHH, :, 2 * bp + pl, m1:m1 + 1]
                                        .rearrange("p dh one -> p (dh one)"),
                                    pp[64 * par:64 * par + NHH, bp * 64 + par:bp * 64 + 64:2]),
                                    deps=[Bpmm])
                    prev[("uev", o, grp, m1)] = last
                    if m1 == 127:
                        prev[("phase_uev",)] = last
                for gc in range(8):
                    gls = []
                    for b in range(4):
                        for par in range(2):
                            gl = p.op("sync", lambda s, gc=gc, grp=grp, gsrc=gsrc, par=par, b=b: s.dma_start(
                                out=Gg.rearrange("p (dh b ll) -> p dh b ll", b=4, ll=128)[64 * par:64 * par + NHH, :, b, :],
                                in_=gsrc[grp * 64 + gc * 8:grp * 64 + (gc + 1) * 8, b, :]
                                    .rearrange("(dh dp) (lh ll) -> dp lh dh ll", dp=2, ll=128)[par]),
                                deps=[prev.get(("gate", o, grp, gc - 1)), prev[("uev", o, grp, 127)]], dma=True)
                            gls.append(gl)
                    gt = p.op("vector", lambda v, gc=gc, udst=udst: v.tensor_mul(
                        udst[:, gc * 2048:(gc + 1) * 2048],
                        udst[:, gc * 2048:(gc + 1) * 2048], Gg[:, :]),
                        deps=gls)
                    prev[("gate", o, grp, gc)] = gt
                if o == 1:
                    for par in range(2):
                        for b in range(4):
                            st = p.op("gpsimd", lambda g, par=par, grp=grp, b=b: g.dma_start(
                                out=VFh[grp * 64:(grp + 1) * 64, b, :]
                                    .rearrange("(dh dp) (lh ll) -> dp lh dh ll", dp=2, ll=128)[par],
                                in_=Mb.rearrange("p (dh b ll) -> p dh b ll", b=4, ll=128)[64 * par:64 * par + NHH, :, b, :]),
                                deps=[prev[("gate", 1, grp, 7)]], dma=True)
                            vf_sts.append(st)
                    prev[("vfst", grp)] = vf_sts[-8:]

        p.op("gpsimd", lambda g: g.nop(), deps=vf_sts)
        p.emit(nc, block, ctx)
    return nc


def k3_consts(gamma=300000.0):
    import ml_dtypes
    N1, NF = 128, 16384
    rho = float(gamma) ** (1.0 / L)
    W128 = np.exp(-2j * np.pi * np.outer(np.arange(N1), np.arange(N1)) / N1)
    wrow = (rho ** (-128.0 * np.arange(64)))[:, None]
    WA_re = W128[:64, :].real * wrow
    WA_im = W128[:64, :].imag * wrow
    WA = np.concatenate([WA_re, WA_im, -WA_im], axis=1)
    WA = np.concatenate([WA, WA], axis=0).astype(ml_dtypes.bfloat16)
    WC128 = np.conj(W128)
    WC = np.concatenate([WC128.real, WC128.imag, -WC128.imag], axis=1).astype(ml_dtypes.bfloat16)
    GB = np.empty((N1 * N1, 384), np.float32)
    WNp = np.exp(-2j * np.pi / NF)
    wn2 = (rho ** (-np.arange(N1, dtype=np.float64)))[:, None]
    for k2 in range(N1):
        G = W128 * (WNp ** (np.arange(N1)[:, None] * k2)) * wn2
        GB[k2 * 128:(k2 + 1) * 128] = np.concatenate([G.real, G.imag, -G.imag], axis=1)
    GC = np.empty((N1 * N1, 384), np.float32)
    wm2 = (rho ** (128.0 * np.arange(64)))[None, :]
    for m1 in range(N1):
        G = (1.0 / NF) * np.conj((WNp ** (np.arange(N1)[:, None] * m1)) * W128[:, :64])
        G = G * wm2 * (rho ** m1)
        gr = np.concatenate([G.real, G.real], axis=1)
        gi = np.concatenate([G.imag, G.imag], axis=1)
        GC[m1 * 128:(m1 + 1) * 128] = np.concatenate([gr, gi, -gi], axis=1)
    return (WA, WC, GB.astype(ml_dtypes.bfloat16), GC.astype(ml_dtypes.bfloat16))


_NC_CACHE = {}


def _get_nc(name, builder):
    if name not in _NC_CACHE:
        _NC_CACHE[name] = builder()
    return _NC_CACHE[name]


def kernel(**inputs):
    import ml_dtypes
    r1 = run_bass_kernel_spmd(_get_nc("k1", build_k1), _k1_inputs(inputs),
                              core_ids=list(range(NCORES))).results
    x = np.asarray(inputs["L0_attn_x"], dtype=np.float32)
    woutT = np.ascontiguousarray(np.asarray(inputs["L0_attn_W_out"]).T.astype(np.float32))
    maps = []
    for c in range(NCORES):
        b, h = c >> 1, c & 1
        carryP = r1[2 * b]["carry"] if h == 1 else np.zeros((128, 12), np.float32)
        maps.append({"wT": woutT, "rhs": r1[c]["Mout"], "carryP": carryP,
                     "extra": r1[c]["attn0"],
                     "xs": np.ascontiguousarray(x[b].T[:, h * HALF:(h + 1) * HALF])})
    r2a = run_bass_kernel_spmd(_get_nc("k2a", lambda: build_mixout(12, True)), maps,
                               core_ids=list(range(NCORES))).results
    maps = _ffn_maps(inputs, "L0_ffn", [r2a[c]["x1o"] for c in range(NCORES)])
    r2b = run_bass_kernel_spmd(_get_nc("k2b", lambda: build_ffn(False)), maps,
                               core_ids=list(range(NCORES))).results
    win1T = np.ascontiguousarray(np.asarray(inputs["L1_attn_W_in"]).T.astype(np.float32))
    maps = [{"x2h": r2b[c]["x2o"], "winT": win1T} for c in range(NCORES)]
    r2c = run_bass_kernel_spmd(_get_nc("k2c", build_proj), maps,
                               core_ids=list(range(NCORES))).results
    proj = np.empty((3072, 4, L), dtype=ml_dtypes.bfloat16)
    for b in range(4):
        for h in range(2):
            proj[:, b, h * HALF:(h + 1) * HALF] = r2c[2 * b + h]["pro"].view(ml_dtypes.bfloat16)
    WA, WC, GB, GC = k3_consts()
    tlv = np.tile(np.linspace(0, 1, L, dtype=np.float32)[None, :], (64, 1))
    f_w1 = np.asarray(inputs["L1_attn_f_w1"], np.float32)
    f_b1 = np.asarray(inputs["L1_attn_f_b1"], np.float32)
    f_w2 = np.asarray(inputs["L1_attn_f_w2"], np.float32)
    f_b2 = np.asarray(inputs["L1_attn_f_b2"], np.float32)
    fw1 = np.zeros((128, 4), np.float32)
    fw1[:64, 0] = f_w1[0, :, 0]; fw1[:64, 1] = f_b1[0]
    fw1[:64, 2] = f_w1[1, :, 0]; fw1[:64, 3] = f_b1[1]
    maps = []
    for c in range(NCORES):
        sl = slice(128 * c, 128 * (c + 1))
        w2t = np.zeros((2, 65, 128), np.float32)
        for o in range(2):
            w2t[o, :64, :] = f_w2[o, sl, :].T
            w2t[o, 64, :] = f_b2[o, sl]
        maps.append({"Vh": np.ascontiguousarray(proj[sl]),
                     "G1h": np.ascontiguousarray(proj[1024 + 128 * c:1024 + 128 * (c + 1)]),
                     "G2h": np.ascontiguousarray(proj[2048 + 128 * c:2048 + 128 * (c + 1)]),
                     "tl": tlv, "fw1": fw1, "w2t": w2t,
                     "WAh": WA, "WCh": WC, "GBh": GB, "GCh": GC})
    r3 = run_bass_kernel_spmd(_get_nc("k3", build_k3), maps,
                              core_ids=list(range(NCORES))).results
    vfin = np.empty((1024, 4, L), dtype=ml_dtypes.bfloat16)
    for c in range(NCORES):
        vfin[128 * c:128 * (c + 1)] = r3[c]["VFh"].view(ml_dtypes.bfloat16)
    wout1T = np.ascontiguousarray(np.asarray(inputs["L1_attn_W_out"]).T.astype(np.float32))
    maps = []
    for c in range(NCORES):
        b, h = c >> 1, c & 1
        maps.append({"wT": wout1T,
                     "rhs": np.ascontiguousarray(vfin[:, b, h * HALF:(h + 1) * HALF]),
                     "xs": r2b[c]["x2o"]})
    r4a = run_bass_kernel_spmd(_get_nc("k4a", lambda: build_mixout(8, False)), maps,
                               core_ids=list(range(NCORES))).results
    maps = _ffn_maps(inputs, "L1_ffn", [r4a[c]["x1o"] for c in range(NCORES)])
    ident = np.eye(128, dtype=np.float32)
    for m in maps:
        m["ident"] = ident
    r4b = run_bass_kernel_spmd(_get_nc("k4b", lambda: build_ffn(True)), maps,
                               core_ids=list(range(NCORES))).results
    out = np.empty((B, L, D), np.float32)
    for c in range(NCORES):
        b, h = c >> 1, c & 1
        out[b, h * HALF:(h + 1) * HALF, :] = r4b[c]["osh"]
    return out
